# version 01 (final)
# baseline (speedup 1.0000x reference)
import numpy as np
import ml_dtypes
import concourse.bass as bass
import concourse.mybir as mybir
from concourse.bass_utils import run_bass_kernel_spmd

F32 = mybir.dt.float32
F16 = mybir.dt.float16
BF16 = mybir.dt.bfloat16
AF = mybir.ActivationFunctionType
ALU = mybir.AluOpType
AX = mybir.AxisListType
LN2C = float(np.log(2.0))

B, T, D = 4, 4096, 1024
PAIRS = 8
HID = 4096
OWN = 2048
NT_OWN = 16
POS_E = [8, 16, 24, 32]
QCS_EVEN = [0, 3, 4, 7]
QCS_ODD = [1, 2, 5, 6]

DEBUG = 0


class Sem:
    def __init__(self, nc, name):
        self.h = nc.alloc_semaphore(name)
        self.tot = 0


class Prog:
    def __init__(self):
        self.ops = []

    def op(self, eng, fn, waits=(), incs=(), drain=False):
        for s, amt in incs:
            s.tot += amt
        self.ops.append((eng, fn, list(waits), list(incs), drain))

    def emit(self, eng_name, eng):
        for e, fn, waits, incs, drain in self.ops:
            if e != eng_name:
                continue
            for s, v in waits:
                if v > 0:
                    eng.wait_ge(s.h, v)
            ins = fn(eng)
            if drain:
                ins = eng.drain()
            for s, amt in incs:
                ins = ins.then_inc(s.h, amt)


def build_program():
    nc = bass.Bass("TRN2", target_bir_lowering=False, debug=False,
                   dynamic_dma_scratch_size=16384, enable_partition_id=False)

    x_kv_h = nc.dram_tensor("x_kv", [T, D], F32, kind="ExternalInput").ap()
    x_q_h = nc.dram_tensor("x_q", [OWN, D], F32, kind="ExternalInput").ap()
    wqkvT_h = nc.dram_tensor("wqkvT", [D, 3 * D], F32, kind="ExternalInput").ap()
    woutT_h = nc.dram_tensor("woutT", [D, D], F32, kind="ExternalInput").ap()
    bout_h = nc.dram_tensor("bout", [1, D], F32, kind="ExternalInput").ap()
    wfc_h = nc.dram_tensor("wfc", [HID, D], F32, kind="ExternalInput").ap()
    bfc_h = nc.dram_tensor("bfc", [1, HID], F32, kind="ExternalInput").ap()
    wpj_h = nc.dram_tensor("wpj", [D, HID], F32, kind="ExternalInput").ap()
    bpj_h = nc.dram_tensor("bpj", [1, D], F32, kind="ExternalInput").ap()
    masks_h = nc.dram_tensor("masks", [128, 4 * 4096], BF16, kind="ExternalInput").ap()
    ident_h = nc.dram_tensor("ident", [128, 128], BF16, kind="ExternalInput").ap()
    eind_h = nc.dram_tensor("eind", [32, 1024], BF16, kind="ExternalInput").ap()
    rows_h = nc.dram_tensor("rowsin", [1, 8192], F32, kind="ExternalInput").ap()

    out_h = nc.dram_tensor("out", [OWN, D], F32, kind="ExternalOutput").ap()

    q_scr = nc.dram_tensor("q_scr", [PAIRS, 4, 128, 512], BF16, kind="ExternalOutput").ap()
    o_scr = nc.dram_tensor("o_scr", [PAIRS, 4, 128, 512], BF16, kind="ExternalOutput").ap()
    wfcq_scr = nc.dram_tensor("wfcq_scr", [128, 8 * HID], BF16, kind="ExternalOutput").ap()
    wpjq_scr = nc.dram_tensor("wpjq_scr", [128, 32 * D], BF16, kind="ExternalOutput").ap()
    if DEBUG == 1:
        kdbg_h = nc.dram_tensor("k_dbg", [128, PAIRS * T], BF16, kind="ExternalOutput").ap()
        vdbg_h = nc.dram_tensor("v_dbg", [128, 32 * 1040], BF16, kind="ExternalOutput").ap()
    if DEBUG == 2:
        rdbg_h = nc.dram_tensor("r_dbg", [32, 128, 512], F32, kind="ExternalOutput").ap()
        rsdbg_h = nc.dram_tensor("rs_dbg", [32, 4, 512], F32, kind="ExternalOutput").ap()
        rcdbg_h = nc.dram_tensor("rc_dbg", [32, 1, 1024], F16, kind="ExternalOutput").ap()
        pdbg_h = nc.dram_tensor("p_dbg", [16, 128, 1024], BF16, kind="ExternalOutput").ap()
        qdbg_h = nc.dram_tensor("q_dbg", [128, 512], BF16, kind="ExternalOutput").ap()
        s2dbg_h = nc.dram_tensor("s2_dbg", [2, 128, 1024], F32, kind="ExternalOutput").ap()
        k2dbg_h = nc.dram_tensor("k2_dbg", [128, PAIRS * T], BF16, kind="ExternalOutput").ap()
        oadbg_h = nc.dram_tensor("oa_dbg", [2, 128, 512], F32, kind="ExternalOutput").ap()
        s3dbg_h = nc.dram_tensor("s3_dbg", [2, 128, 1024], F32, kind="ExternalOutput").ap()
        pvdbg_h = nc.dram_tensor("pv_dbg", [2, 128, 640], F32, kind="ExternalOutput").ap()
        v2dbg_h = nc.dram_tensor("v2_dbg", [128, 32 * 1040], BF16, kind="ExternalOutput").ap()

    alloc = lambda n, s, d: nc.alloc_sbuf_tensor(n, s, d)[:]
    A16 = alloc("A16", [128, 66048], BF16)
    K_fm = A16[:, 0:PAIRS * T]
    V_aug = A16[:, PAIRS * T:66048]
    wfcq_fm = A16[:, 0:PAIRS * HID]
    wpjq_fm = A16[:, PAIRS * HID:PAIRS * HID + 32 * D]

    WA = alloc("WA", [128, 16384], BF16)
    wkv_bf = WA[:, 0:16384]
    wq_bf = WA[:, 0:8192]
    masks_sb = WA[:, 0:16384]
    woutT_bf = WA[:, 0:8192]
    bout_bf = WA[0:1, 8192:9216]
    bfc_bf = WA[0:1, 9216:13312]
    bpj_bf = WA[0:1, 13312:14336]
    x1_sb = WA[:, 14336:16384].bitcast(F32)

    SC = alloc("SC", [128, 10944], BF16)
    t16 = SC[:, 0:1024].bitcast(F16)
    m16 = SC[:, 1024:2048].bitcast(F16)
    wa16 = SC[:, 2048:3072].bitcast(F16)
    wb16 = SC[:, 3072:4096].bitcast(F16)
    rq_sb = SC[:, 4096:5120]
    qout_sb = SC[:, 5120:6144]
    sTb_sb = SC[0:32, 6144:6272]
    snegb_sb = SC[:, 6272:6336]
    xn_tm = SC[:, 0:1024]
    xn_grp = SC[:, 1024:5120]
    qe_sb = SC[:, 5120:5632]
    p_sb = SC[:, 6336:8384]
    pm_sb = SC[:, 8384:9408]
    qt_sb = SC[:, 9408:10432]
    o_sb = SC[:, 10432:10944]
    rc16 = SC[0:1, 8384:9408].bitcast(F16)
    yq_sb = SC[:, 6336:7360]
    gqT_sb = SC[:, 7360:8384]
    ot_sb = SC[:, 8384:10432]

    ident_sb = alloc("ident_sb", [128, 128], BF16)
    eind_sb = alloc("eind_sb", [32, 1024], BF16)

    A32 = alloc("A32", [128, 5120], F32)
    XO = [A32[:, i * 1024:(i + 1) * 1024] for i in range(2)]
    sqs_sb = A32[:, 2048:3072]
    y_sb = A32[:, 2048:3072]
    sfm_sb = A32[:, 2048:3072]
    gw_sb = A32[:, 3072:5120]
    R_sb = XO[0][:, 0:512]
    rows32 = gw_sb[0:1, 0:2048]

    SM = alloc("SM", [128, 273], F32)
    ones_bf = SM[0:1, 177:241].bitcast(BF16)
    ones16_sb = SM[0:1, 241:273].bitcast(F16)
    st_sb = SM[:, 0:16]
    eps_sb = SM[:, 16:17]
    amax_sb = SM[:, 17:49]
    usm_sb = SM[:, 49:81]
    sa_sb = SM[:, 81:113]
    sb2_sb = SM[:, 113:145]
    sm_sb = SM[:, 145:177]
    fneg_sb = SM[:, 17:49]
    invs_sb = SM[:, 145:177]
    sneg_sb = SM[:, 81:113]

    ps = lambda n, s, d: nc.alloc_psum_tensor(n, s, d)[:]
    PS_A = ps("PS_A", [128, 1024], F32)
    PS_B = ps("PS_B", [128, 1024], F32)
    PS_C = ps("PS_C", [128, 512], F32)
    PS_D = ps("PS_D", [128, 512], F32)
    PS_E = ps("PS_E", [128, 1024], BF16)
    PS_F = ps("PS_F", [128, 512], F32)

    c_pe = Sem(nc, "c_pe")
    c_act = Sem(nc, "c_act")
    c_dve = Sem(nc, "c_dve")
    s_sy = Sem(nc, "s_sy")
    s_gp = Sem(nc, "s_gp")
    s_sc = Sem(nc, "s_sc")

    P = Prog()
    W = lambda s: (s, s.tot)

    P.op("scalar", lambda e: e.dma_start(out=ident_sb, in_=ident_h), incs=[(s_sc, 16)])
    P.op("scalar", lambda e: e.dma_start(out=eind_sb, in_=eind_h), incs=[(s_sc, 16)])

    P.op("vector", lambda e: e.memset(eps_sb, 1e-5))
    P.op("vector", lambda e: e.memset(
        V_aug.rearrange("p (nt s c) -> p (nt s) c", s=16, c=65)[:, :, 64:65], 1.0),
        incs=[(c_dve, 1)])
    vone_done = c_dve.tot
    P.op("scalar", lambda e: e.dma_start(
        out=rows32[0:1, 0:192], in_=rows_h[0:1, 6144:6336]), incs=[(s_sc, 16)])
    P.op("vector", lambda e: e.tensor_copy(ones_bf, rows32[0:1, 0:128]),
         waits=[(s_sc, s_sc.tot)], drain=True)
    P.op("vector", lambda e: e.tensor_copy(ones16_sb, rows32[0:1, 128:192]),
         drain=True, incs=[(c_dve, 1)])
    consts_dve = c_dve.tot
    for i in range(16):
        dc, half = divmod(i, 2)
        sl = i % 2
        P.op("gpsimd", lambda e, dc=dc, h=half, sl=sl: e.dma_start(
            out=gw_sb[:, sl * 1024:sl * 1024 + 1024],
            in_=wqkvT_h[128 * dc:128 * dc + 128, D + 1024 * h:D + 1024 * h + 1024]),
            waits=[(c_dve, max(consts_dve, c_dve.tot - 1))], incs=[(s_gp, 16)])
        P.op("vector", lambda e, dc=dc, h=half, sl=sl: e.tensor_copy(
            wkv_bf[:, 2048 * dc + 1024 * h:2048 * dc + 1024 * h + 1024],
            gw_sb[:, sl * 1024:sl * 1024 + 1024]),
            waits=[(s_gp, s_gp.tot)], incs=[(c_dve, 1)])
    wkv_ready = c_dve.tot

    def ln_tile(xin, xn_out, slot, extra_act_waits=()):
        base = 8 * slot
        c = lambda i: st_sb[:, base + i:base + i + 1]
        P.op("vector", lambda e: e.reduce_sum(c(0), xin, axis=AX.X), drain=True)
        P.op("vector", lambda e: e.scalar_tensor_tensor(
            sqs_sb, xin, 1.0, xin, op0=ALU.mult, op1=ALU.mult), drain=True)
        P.op("vector", lambda e: e.reduce_sum(c(1), sqs_sb, axis=AX.X), drain=True)
        P.op("vector", lambda e: e.tensor_scalar_mul(c(2), c(0), 1.0 / 1024.0),
             drain=True)
        P.op("vector", lambda e: e.scalar_tensor_tensor(
            c(3), c(2), 1.0, c(2), op0=ALU.mult, op1=ALU.mult), drain=True)
        P.op("vector", lambda e: e.scalar_tensor_tensor(
            c(4), c(1), 1.0 / 1024.0, c(3), op0=ALU.mult, op1=ALU.subtract),
            drain=True, incs=[(c_dve, 1)])
        P.op("scalar", lambda e: e.activation(c(5), c(4), AF.Sqrt, bias=eps_sb),
             waits=[W(c_dve)], incs=[(c_act, 1)])
        P.op("vector", lambda e: e.reciprocal(c(6), c(5)),
             waits=[W(c_act)], drain=True)
        P.op("vector", lambda e: e.scalar_tensor_tensor(
            c(7), c(2), -1.0, c(6), op0=ALU.mult, op1=ALU.mult),
            drain=True, incs=[(c_dve, 1)])
        P.op("scalar", lambda e: e.activation(xn_out, xin, AF.Identity,
                                              scale=c(6), bias=c(7)),
             waits=[W(c_dve)] + list(extra_act_waits), incs=[(c_act, 1)])

    state = dict(rq_free_pe=0, qout_dma=0, qout_dma_sem=None)

    def quant_chain(vin, slot):
        vblk = vin.rearrange("p (nb k) -> p nb k", k=32)
        P.op("vector", lambda e: e.tensor_reduce(
            amax_sb, vblk, axis=AX.X, op=ALU.max, apply_absolute_value=True),
            drain=True)
        P.op("vector", lambda e: e.tensor_scalar_max(
            usm_sb, amax_sb, float(2.0 ** -40)), drain=True, incs=[(c_dve, 1)])
        P.op("scalar", lambda e: e.activation(amax_sb, usm_sb, AF.Ln),
             waits=[W(c_dve)], incs=[(c_act, 1)])
        P.op("vector", lambda e: e.tensor_scalar(
            usm_sb, amax_sb, 1.0 / LN2C, 41.0, op0=ALU.mult, op1=ALU.add),
            waits=[W(c_act)], drain=True)
        P.op("vector", lambda e: e.tensor_copy(sa_sb, usm_sb), drain=True)
        cur, nxt = sa_sb, sb2_sb
        for k in [5, 4, 3, 2, 1, 0]:
            P.op("vector", lambda e, c=cur, k=k: e.tensor_scalar(
                sm_sb, c, float(2.0 ** k), float(2.0 ** k),
                op0=ALU.is_ge, op1=ALU.mult), drain=True)
            P.op("vector", lambda e, c=cur, n=nxt: e.tensor_sub(n, c, sm_sb),
                 drain=True)
            cur, nxt = nxt, cur
        P.op("vector", lambda e, c=cur: e.scalar_tensor_tensor(
            fneg_sb, c, 44.0, usm_sb, op0=ALU.add, op1=ALU.subtract),
            drain=True, incs=[(c_dve, 1)])
        P.op("scalar", lambda e: e.activation(invs_sb, fneg_sb, AF.Exp, scale=LN2C),
             waits=[W(c_dve)])
        P.op("scalar", lambda e: e.activation(sneg_sb, fneg_sb, AF.Exp, scale=-LN2C),
             incs=[(c_act, 1)])
        P.op("vector", lambda e: e.tensor_scalar_mul(sneg_sb, sneg_sb, -1.0),
             waits=[W(c_act)], drain=True)
        P.op("vector", lambda e, sl=slot: e.tensor_copy(
            snegb_sb[:, 32 * sl:32 * sl + 32], sneg_sb), drain=True,
            incs=[(c_dve, 1)])
        invs_b = invs_sb.unsqueeze(2).broadcast_to([128, 32, 32])
        P.op("vector", lambda e: e.tensor_tensor(
            t16.rearrange("p (nb k) -> p nb k", k=32), vblk, invs_b, op=ALU.mult),
            drain=True)
        P.op("vector", lambda e: e.tensor_copy(wa16, t16), drain=True)
        cur16, nxt16 = wa16, wb16
        for k in [5, 4, 3, 2, 1, 0]:
            P.op("vector", lambda e, c=cur16, k=k: e.tensor_scalar(
                m16, c, float(2.0 ** k) - 16.5, float(2.0 ** k),
                op0=ALU.is_ge, op1=ALU.mult), drain=True)
            P.op("vector", lambda e, c=cur16, n=nxt16: e.tensor_sub(n, c, m16),
                 drain=True)
            cur16, nxt16 = nxt16, cur16
        P.op("vector", lambda e, c=cur16: e.scalar_tensor_tensor(
            rq_sb, c, 16.0, t16, op0=ALU.add, op1=ALU.subtract),
            waits=[(c_pe, state["rq_free_pe"])], drain=True, incs=[(c_dve, 1)])

    def quantT(slot, store=None):
        rdy = c_dve.tot
        P.op("tensor", lambda e, sl=slot: e.transpose(
            PS_E[0:32, 0:128], snegb_sb[:, 32 * sl:32 * sl + 32], ident_sb),
            waits=[(c_dve, rdy)], incs=[(c_pe, 1)])
        P.op("vector", lambda e: e.tensor_copy(sTb_sb, PS_E[0:32, 0:128]),
            waits=[W(c_pe)], drain=True, incs=[(c_dve, 1)])
        sT_rdy = c_dve.tot
        for rnd in range(2):
            def exp_mms(e, rnd=rnd):
                last = None
                for ci in range(4):
                    last = e.matmul(PS_F[:, 128 * ci:128 * ci + 128],
                                    eind_sb[:, 128 * (4 * rnd + ci):128 * (4 * rnd + ci) + 128],
                                    sTb_sb, start=True, stop=True)
                return last
            P.op("tensor", exp_mms, waits=[(c_dve, sT_rdy if rnd == 0 else c_dve.tot)],
                 incs=[(c_pe, 1)])
            P.op("vector", lambda e, rnd=rnd: e.tensor_copy(
                sfm_sb[:, 512 * rnd:512 * rnd + 512], PS_F),
                waits=[W(c_pe)], drain=True, incs=[(c_dve, 1)])
        sfm_rdy = c_dve.tot
        def r_T(e):
            last = None
            for ci in range(8):
                last = e.transpose(PS_E[:, 128 * ci:128 * ci + 128],
                                   rq_sb[:, 128 * ci:128 * ci + 128], ident_sb)
            return last
        P.op("tensor", r_T, waits=[(c_dve, sfm_rdy)], incs=[(c_pe, 1)])
        state["rq_free_pe"] = c_pe.tot
        waits = [W(c_pe)]
        if state["qout_dma_sem"] is not None:
            waits.append((state["qout_dma_sem"], state["qout_dma"]))
        P.op("vector", lambda e: e.tensor_tensor(
            qout_sb, PS_E, sfm_sb, op=ALU.mult),
            waits=waits, drain=True, incs=[(c_dve, 1)])
        if store is not None:
            P.op("gpsimd", store, waits=[W(c_dve)], incs=[(s_gp, 16)])
            state["qout_dma"] = s_gp.tot
            state["qout_dma_sem"] = s_gp
        return c_dve.tot

    x_free = [0, 0]
    xnT_prev = [0]
    kq_bank_evac = [0, 0]
    v_bank_evac = [0, 0]
    for g in range(8):
        for t in range(4):
            tile = 4 * g + t
            slot = tile % 2
            P.op("sync", lambda e, tile=tile, sl=slot: e.dma_start(
                out=XO[sl], in_=x_kv_h[128 * tile:128 * tile + 128, :]),
                waits=[(c_act, x_free[slot]),
                       (c_dve, consts_dve if tile < 2 else 0)],
                incs=[(s_sy, 16)])
            P.op("vector", lambda e: e.engine_nop(), waits=[(s_sy, s_sy.tot)])
            ln_tile(XO[slot], xn_tm, slot,
                    extra_act_waits=[(c_pe, xnT_prev[0])])
            x_free[slot] = c_act.tot
            def xnT(e):
                last = None
                for dc in range(8):
                    last = e.transpose(PS_E[:, 128 * dc:128 * dc + 128],
                                       xn_tm[:, 128 * dc:128 * (dc + 1)], ident_sb)
                return last
            P.op("tensor", xnT, waits=[W(c_act), (c_dve, c_dve.tot)], incs=[(c_pe, 1)])
            xnT_prev[0] = c_pe.tot
            P.op("vector", lambda e, t=t: e.tensor_copy(
                xn_grp.rearrange("p (dc c) -> p dc c", dc=8)[:, :, 128 * t:128 * t + 128],
                PS_E.rearrange("p (dc c) -> p dc c", dc=8)),
                waits=[W(c_pe)], drain=True, incs=[(c_dve, 1)])
        grp_rdy = c_dve.tot
        for pc in range(PAIRS):
            bank, bi = (PS_C, 0) if pc % 2 == 0 else (PS_D, 1)
            def kmm(e, pc=pc, bank=bank):
                last = None
                for dc in range(8):
                    last = e.matmul(bank,
                                    wkv_bf[:, 2048 * dc + 128 * pc:2048 * dc + 128 * pc + 128],
                                    xn_grp[:, 512 * dc:512 * (dc + 1)],
                                    start=(dc == 0), stop=(dc == 7))
                return last
            P.op("tensor", kmm,
                 waits=[(c_dve, max(grp_rdy, wkv_ready)), (c_act, kq_bank_evac[bi])],
                 incs=[(c_pe, 1)])
            P.op("scalar", lambda e, pc=pc, g=g, bank=bank: e.activation(
                K_fm[:, T * pc + 512 * g:T * pc + 512 * g + 512], bank, AF.Identity),
                waits=[W(c_pe)], incs=[(c_act, 1)])
            kq_bank_evac[bi] = c_act.tot
        for t in range(4):
            kt = 4 * g + t
            vbank, vbi = (PS_A, 0) if t % 2 == 0 else (PS_B, 1)
            def vmm(e, t=t, vb=vbank):
                last = None
                for dc in range(8):
                    for half in range(2):
                        last = e.matmul(
                            vb[:, 512 * half:512 * half + 512],
                            xn_grp[:, 512 * dc + 128 * t:512 * dc + 128 * t + 128],
                            wkv_bf[:, 2048 * dc + 1024 + 512 * half:2048 * dc + 1024 + 512 * (half + 1)],
                            start=(dc == 0), stop=(dc == 7))
                return last
            P.op("tensor", vmm, waits=[(c_act, v_bank_evac[vbi])], incs=[(c_pe, 1)])
            vdst = V_aug[:, 1040 * kt:1040 * kt + 1040].rearrange(
                "p (s c) -> p s c", c=65)[:, :, 0:64]
            P.op("scalar", lambda e, vb=vbank, vdst=vdst: e.activation(
                vdst, vb.rearrange("p (s c) -> p s c", c=64), AF.Identity),
                waits=[W(c_pe), (c_dve, vone_done)], incs=[(c_act, 1)])
            v_bank_evac[vbi] = c_act.tot
    p1a_pe = c_pe.tot

    for dc in range(8):
        sl = dc % 2
        P.op("gpsimd", lambda e, dc=dc, sl=sl: e.dma_start(
            out=gw_sb[:, sl * 1024:sl * 1024 + 1024],
            in_=wqkvT_h[128 * dc:128 * dc + 128, 0:1024]),
            waits=[(c_pe, p1a_pe), (c_dve, c_dve.tot)], incs=[(s_gp, 16)])
        P.op("vector", lambda e, dc=dc, sl=sl: e.tensor_copy(
            wq_bf[:, 1024 * dc:1024 * dc + 1024], gw_sb[:, sl * 1024:sl * 1024 + 1024]),
            waits=[(s_gp, s_gp.tot)], incs=[(c_dve, 1)])
    wq_ready = c_dve.tot
    qe_free = [0]
    for g in range(4):
        for t in range(4):
            tile = 4 * g + t
            slot = tile % 2
            P.op("sync", lambda e, tile=tile, sl=slot: e.dma_start(
                out=XO[sl], in_=x_q_h[128 * tile:128 * tile + 128, :]),
                waits=[(c_act, x_free[slot])], incs=[(s_sy, 16)])
            P.op("vector", lambda e: e.engine_nop(), waits=[(s_sy, s_sy.tot)])
            ln_tile(XO[slot], xn_tm, slot,
                    extra_act_waits=[(c_pe, xnT_prev[0])])
            x_free[slot] = c_act.tot
            def xnT(e):
                last = None
                for dc in range(8):
                    last = e.transpose(PS_E[:, 128 * dc:128 * dc + 128],
                                       xn_tm[:, 128 * dc:128 * (dc + 1)], ident_sb)
                return last
            P.op("tensor", xnT, waits=[W(c_act)], incs=[(c_pe, 1)])
            xnT_prev[0] = c_pe.tot
            P.op("vector", lambda e, t=t: e.tensor_copy(
                xn_grp.rearrange("p (dc c) -> p dc c", dc=8)[:, :, 128 * t:128 * t + 128],
                PS_E.rearrange("p (dc c) -> p dc c", dc=8)),
                waits=[W(c_pe)], drain=True, incs=[(c_dve, 1)])
        grp_rdy = c_dve.tot
        for pc in range(PAIRS):
            bank, bi = (PS_C, 0) if pc % 2 == 0 else (PS_D, 1)
            def qmm(e, pc=pc, bank=bank):
                last = None
                for dc in range(8):
                    last = e.matmul(bank,
                                    wq_bf[:, 1024 * dc + 128 * pc:1024 * dc + 128 * pc + 128],
                                    xn_grp[:, 512 * dc:512 * (dc + 1)],
                                    start=(dc == 0), stop=(dc == 7))
                return last
            P.op("tensor", qmm,
                 waits=[(c_dve, max(grp_rdy, wq_ready)), (c_act, kq_bank_evac[bi])],
                 incs=[(c_pe, 1)])
            P.op("scalar", lambda e, bank=bank: e.activation(
                qe_sb, bank, AF.Identity, scale=0.125),
                waits=[W(c_pe), (s_sy, qe_free[0])], incs=[(c_act, 1)])
            kq_bank_evac[bi] = c_act.tot
            P.op("sync", lambda e, pc=pc, g=g: e.dma_start(
                out=q_scr[pc, g], in_=qe_sb),
                waits=[W(c_act)], incs=[(s_sy, 16)])
            qe_free[0] = s_sy.tot
    p1_pe = c_pe.tot
    p1_act = c_act.tot
    p1_sy = s_sy.tot

    if DEBUG == 1:
        P.op("sync", lambda e: e.dma_start(out=kdbg_h, in_=K_fm),
             waits=[(c_act, p1_act)], incs=[(s_sy, 16)])
        P.op("sync", lambda e: e.dma_start(out=vdbg_h, in_=V_aug), incs=[(s_sy, 16)])
        P.op("sync", lambda e: e.nop(), waits=[(s_sy, s_sy.tot)])
        return nc, P

    P.op("scalar", lambda e: e.dma_start(out=masks_sb, in_=masks_h),
         waits=[(c_pe, p1_pe)], incs=[(s_sc, 16)])
    masks_rdy = s_sc.tot
    P.op("vector", lambda e: e.memset(qt_sb[64:128, 0:512], 0.0),
         waits=[(c_pe, p1_pe)])
    P.op("vector", lambda e: e.memset(qt_sb[0:64, 512:1024], 0.0),
         drain=True, incs=[(c_dve, 1)])
    qtpad_rdy = c_dve.tot

    wq_items = [("fc", i) for i in range(32)] + [("pj", i) for i in range(32)]
    wq_i = [0]
    wld_dve = [c_dve.tot, c_dve.tot]

    def emit_wquant_chunk():
        import os
        if os.environ.get("NOWQ"):
            return
        if wq_i[0] >= len(wq_items):
            return
        kind, idx = wq_items[wq_i[0]]
        wq_i[0] += 1
        sl = wq_i[0] % 2
        if kind == "fc":
            src = wfc_h[128 * idx:128 * idx + 128, :]
        else:
            r, cc = divmod(idx, 4)
            src = wpj_h[128 * r:128 * r + 128, 1024 * cc:1024 * cc + 1024]
        P.op("gpsimd", lambda e, src=src, sl=sl: e.dma_start(
            out=gw_sb[:, sl * 1024:sl * 1024 + 1024], in_=src),
            waits=[(c_dve, wld_dve[sl])], incs=[(s_gp, 16)])
        P.op("vector", lambda e: e.engine_nop(), waits=[(s_gp, s_gp.tot)])
        import os as _os
        if _os.environ.get("WQLITE"):
            waits = [W(c_pe)]
            if state["qout_dma_sem"] is not None:
                waits.append((state["qout_dma_sem"], state["qout_dma"]))
            P.op("vector", lambda e, sl=sl: e.tensor_copy(
                qout_sb, gw_sb[:, sl * 1024:sl * 1024 + 1024]),
                waits=waits, drain=True, incs=[(c_dve, 1)])
            wld_dve[sl] = c_dve.tot
            if kind == "fc":
                dst = wfcq_scr.rearrange("p (dc c) -> p dc c", dc=8)[:, :, 128 * idx:128 * idx + 128]
            else:
                r, cc = divmod(idx, 4)
                dst = wpjq_scr.rearrange("p (hc c) -> p hc c", hc=32)[:, 8 * cc:8 * cc + 8, 128 * r:128 * r + 128]
            P.op("gpsimd", lambda e, dst=dst: e.dma_start(
                out=dst, in_=qout_sb.rearrange("p (dc c) -> p dc c", dc=8)),
                waits=[W(c_dve)], incs=[(s_gp, 16)])
            state["qout_dma"] = s_gp.tot
            state["qout_dma_sem"] = s_gp
            return
        quant_chain(gw_sb[:, sl * 1024:sl * 1024 + 1024], sl)
        wld_dve[sl] = c_dve.tot
        if kind == "fc":
            dst = wfcq_scr.rearrange("p (dc c) -> p dc c", dc=8)[:, :, 128 * idx:128 * idx + 128]
        else:
            r, cc = divmod(idx, 4)
            dst = wpjq_scr.rearrange("p (hc c) -> p hc c", hc=32)[:, 8 * cc:8 * cc + 8, 128 * r:128 * r + 128]
        quantT(sl, store=lambda e, dst=dst: e.dma_start(
            out=dst, in_=qout_sb.rearrange("p (dc c) -> p dc c", dc=8)))

    qt_free = [0]
    o_free = [0]
    rc_free = [0]
    last_exp = [0]
    s_bank_exp = [0, 0]
    p_slot_av = [0, 0]
    pm_av = [0]
    o_evac_prev = [0]
    for pos in range(4):
        E = POS_E[pos]
        for pair in range(PAIRS):
            it = pos * PAIRS + pair
            P.op("scalar", lambda e, pair=pair, pos=pos: e.dma_start(
                out=qt_sb[0:64, 0:512], in_=q_scr[pair, pos][0:64, :]),
                waits=[(c_pe, qt_free[0]), (s_sy, p1_sy), (c_dve, qtpad_rdy)],
                incs=[(s_sc, 16)])
            P.op("scalar", lambda e, pair=pair, pos=pos: e.dma_start(
                out=qt_sb[64:128, 512:1024], in_=q_scr[pair, pos][64:128, :]),
                incs=[(s_sc, 16)])
            import os as _os
            if _os.environ.get("QSETTLE"):
                P.op("vector", lambda e: e.engine_nop(),
                     waits=[(s_sc, s_sc.tot)], drain=True, incs=[(c_dve, 1)])
                P.op("vector", lambda e: e.tensor_copy(XO[1][:, 0:512], qt_sb[:, 0:512]),
                     drain=True, incs=[(c_dve, 1)])
                q_rdy_dve = c_dve.tot
                if DEBUG == 2 and it in (8, 10):
                    P.op("vector", lambda e: e.tensor_copy(x1_sb, qt_sb),
                         drain=True, incs=[(c_dve, 1)])
                    P.op("sync", lambda e, d=(0 if it == 8 else 1): e.dma_start(
                        out=s3dbg_h[d], in_=x1_sb),
                        waits=[(c_dve, c_dve.tot)], incs=[(s_sy, 16)])
                    P.op("vector", lambda e: e.engine_nop(),
                         waits=[(s_sy, s_sy.tot)], incs=[(c_dve, 1)])
                    q_rdy_dve = c_dve.tot
            q_rdy = s_sc.tot
            if DEBUG == 2 and it == 8:
                P.op("sync", lambda e: e.dma_start(out=qdbg_h, in_=qt_sb[:, 0:512]),
                     waits=[(s_sc, q_rdy)], incs=[(s_sy, 16)])
            exp_cnt = {}
            av_cnt = {}
            for kt in range(E):
                sbank, sbi = (PS_A, 0) if kt % 2 == 0 else (PS_B, 1)
                pslot = kt % 2
                waits = [(s_sc, q_rdy), (c_act, max(s_bank_exp[sbi], p1_act))]
                if _os.environ.get("QSETTLE"):
                    waits.append((c_dve, q_rdy_dve))
                if kt == 0:
                    waits.append((c_dve, o_evac_prev[0]))
                import os as _os
                if kt == 0 and pair == 0 and _os.environ.get("BARRIER"):
                    waits += [(c_dve, c_dve.tot), (c_act, c_act.tot),
                              (s_sy, s_sy.tot), (s_gp, s_gp.tot), (s_sc, s_sc.tot)]
                def smm(e, pair=pair, kt=kt, sb_=sbank):
                    import os as _os
                    if _os.environ.get("SWAPS"):
                        e.matmul(sb_[:, 512:1024],
                                 K_fm[:, T * pair + 128 * kt:T * pair + 128 * kt + 128],
                                 qt_sb[:, 0:512], start=True, stop=True)
                        return e.matmul(sb_[:, 0:512],
                                        K_fm[:, T * pair + 128 * kt:T * pair + 128 * kt + 128],
                                        qt_sb[:, 512:1024], start=True, stop=True)
                    if _os.environ.get("BFIRST"):
                        e.matmul(sb_[:, 512:1024],
                                 K_fm[:, T * pair + 128 * kt:T * pair + 128 * kt + 128],
                                 qt_sb[:, 512:1024], start=True, stop=True)
                        return e.matmul(sb_[:, 0:512],
                                        K_fm[:, T * pair + 128 * kt:T * pair + 128 * kt + 128],
                                        qt_sb[:, 0:512], start=True, stop=True)
                    e.matmul(sb_[:, 0:512],
                             K_fm[:, T * pair + 128 * kt:T * pair + 128 * kt + 128],
                             qt_sb[:, 0:512], start=True, stop=True)
                    return e.matmul(sb_[:, 512:1024],
                                    K_fm[:, T * pair + 128 * kt:T * pair + 128 * kt + 128],
                                    qt_sb[:, 512:1024], start=True, stop=True)
                import os as _os2
                if DEBUG == 2 and it in (8, 10) and kt == 0:
                    didx2 = 0 if it == 8 else 1
                    def pe_view(e, pair=pair, kt=kt):
                        e.matmul(PS_F[:, 0:128],
                                 K_fm[:, T * pair + 128 * kt:T * pair + 128 * kt + 128],
                                 ident_sb, start=True, stop=True)
                        return e.matmul(PS_E[:, 0:512].bitcast(F32)[:, 0:256] if False else PS_F[:, 128:256],
                                        ident_sb, qt_sb[:, 0:128], start=True, stop=True)
                    P.op("tensor", pe_view, waits=list(waits), incs=[(c_pe, 1)])
                    P.op("vector", lambda e: e.tensor_copy(x1_sb[:, 0:256], PS_F[:, 0:256]),
                         waits=[W(c_pe)], drain=True, incs=[(c_dve, 1)])
                    P.op("sync", lambda e, d=didx2: e.dma_start(
                        out=pvdbg_h[d][:, 0:256], in_=x1_sb[:, 0:256]),
                        waits=[W(c_dve)], incs=[(s_sy, 16)])
                    P.op("vector", lambda e: e.engine_nop(), waits=[(s_sy, s_sy.tot)],
                         incs=[(c_dve, 1)])
                if _os2.environ.get("SPACED"):
                    def smmA(e, pair=pair, kt=kt, sb_=sbank):
                        return e.matmul(sb_[:, 0:512],
                                 K_fm[:, T * pair + 128 * kt:T * pair + 128 * kt + 128],
                                 qt_sb[:, 0:512], start=True, stop=True)
                    def smmB(e, pair=pair, kt=kt, sb_=sbank):
                        return e.matmul(sb_[:, 512:1024],
                                 K_fm[:, T * pair + 128 * kt:T * pair + 128 * kt + 128],
                                 qt_sb[:, 512:1024], start=True, stop=True)
                    P.op("tensor", smmA, waits=waits, incs=[(c_pe, 1)])
                    P.op("vector", lambda e: e.engine_nop(), waits=[W(c_pe)],
                         incs=[(c_dve, 1)])
                    P.op("tensor", smmB, waits=[W(c_dve)], incs=[(c_pe, 1)])
                    P.op("vector", lambda e: e.engine_nop(), waits=[W(c_pe)],
                         incs=[(c_dve, 1)])
                    smm_settled = c_dve.tot
                else:
                    P.op("tensor", smm, waits=waits, incs=[(c_pe, 1)])
                    P.op("vector", lambda e: e.engine_nop(), waits=[W(c_pe)],
                         incs=[(c_dve, 1)])
                    smm_settled = c_dve.tot
                P.op("scalar", lambda e, sb_=sbank, ps=pslot: e.activation(
                    p_sb[:, ps * 1024:ps * 1024 + 1024], sb_, AF.Exp),
                    waits=[(c_dve, smm_settled), (s_sc, masks_rdy),
                           (c_pe, p_slot_av[pslot])],
                    incs=[(c_act, 1)], drain=True)
                exp_cnt[kt] = c_act.tot
                s_bank_exp[sbi] = c_act.tot
                last_exp[0] = c_act.tot
                if DEBUG == 2 and it == 8:
                    P.op("sync", lambda e, kt=kt, psl=pslot: e.dma_start(
                        out=pdbg_h[kt], in_=p_sb[:, psl * 1024:psl * 1024 + 1024]),
                        waits=[(c_act, exp_cnt[kt])], incs=[(s_sy, 16)])

                import os as _os
                masked = (kt >= E - 8) and not _os.environ.get("NOMASK")
                if masked:
                    jw = kt - (E - 8)
                    P.op("vector", lambda e, ps=pslot, pos=pos, jw=jw: e.tensor_tensor(
                        pm_sb.rearrange("p (h c) -> p h c", h=2),
                        p_sb[:, ps * 1024:ps * 1024 + 1024].rearrange("p (h c) -> p h c", h=2),
                        masks_sb[:, 4096 * pos + 512 * jw:4096 * pos + 512 * (jw + 1)]
                            .unsqueeze(1).broadcast_to([128, 2, 512]),
                        op=ALU.mult),
                        waits=[(c_act, exp_cnt[kt]), (c_pe, pm_av[0])],
                        drain=True, incs=[(c_dve, 1)])
                    av_wait = [(c_dve, c_dve.tot)]
                    av_src = pm_sb
                else:
                    av_wait = [(c_act, exp_cnt[kt])]
                    av_src = p_sb[:, pslot * 1024:pslot * 1024 + 1024]
                def avmm(e, pair=pair, kt=kt, src=av_src, E=E):
                    import os as _os
                    ao, bo = (512, 0) if _os.environ.get("SWAPS") else (0, 512)
                    if _os.environ.get("UNIFWL"):
                        offa = 1040 * kt + 130 * pair
                        offb = offa + 65
                        wa = 128
                        wb = 128 if offb + 128 <= 33280 else 65
                        e.matmul(PS_C[0:wa, :], V_aug[:, offa:offa + wa],
                                 src[:, ao:ao + 512], start=(kt == 0), stop=(kt == E - 1))
                        return e.matmul(PS_D[0:wb, :], V_aug[:, offb:offb + wb],
                                        src[:, bo:bo + 512], start=(kt == 0), stop=(kt == E - 1))
                    e.matmul(PS_C[0:65, :],
                             V_aug[:, 1040 * kt + 130 * pair:1040 * kt + 130 * pair + 65],
                             src[:, ao:ao + 512], start=(kt == 0), stop=(kt == E - 1))
                    return e.matmul(PS_D[0:65, :],
                                    V_aug[:, 1040 * kt + 130 * pair + 65:1040 * kt + 130 * pair + 130],
                                    src[:, bo:bo + 512], start=(kt == 0), stop=(kt == E - 1))
                P.op("tensor", avmm, waits=av_wait, incs=[(c_pe, 1)])
                av_cnt[kt] = c_pe.tot

                p_slot_av[pslot] = c_pe.tot
                if masked:
                    pm_av[0] = c_pe.tot
            qt_free[0] = c_pe.tot
            if DEBUG == 2:
                P.op("vector", lambda e: e.tensor_copy(R_sb[0:1, 0:512], PS_C[64:65, :]),
                     waits=[(c_pe, av_cnt[E - 1])], drain=True)
                P.op("vector", lambda e: e.tensor_copy(R_sb[32:33, 0:512], PS_D[64:65, :]),
                     drain=True)
                P.op("vector", lambda e: e.tensor_copy(R_sb[64:65, 0:512], PS_C[0:1, :]),
                     drain=True)
                P.op("vector", lambda e: e.tensor_copy(R_sb[96:97, 0:512], PS_D[0:1, :]),
                     drain=True, incs=[(c_dve, 1)])
                for j, pp in enumerate([0, 32, 64, 96]):
                    P.op("sync", lambda e, it=it, j=j, pp=pp: e.dma_start(
                        out=rsdbg_h[it, j], in_=R_sb[pp:pp + 1, 0:512]),
                        waits=[W(c_dve)] if j == 0 else [], incs=[(s_sy, 16)])
            P.op("vector", lambda e: e.reciprocal(rc16[0:1, 0:512], PS_C[64:65, :]),
                waits=[(c_pe, av_cnt[E - 1]), (c_pe, rc_free[0])] +
                      ([(s_sy, s_sy.tot)] if DEBUG == 2 else []), drain=True)
            P.op("vector", lambda e: e.reciprocal(rc16[0:1, 512:1024], PS_D[64:65, :]),
                drain=True, incs=[(c_dve, 1)])
            if DEBUG == 2:
                P.op("sync", lambda e, it=it: e.dma_start(
                    out=rcdbg_h[it], in_=rc16),
                    waits=[W(c_dve)], incs=[(s_sy, 16)])
            import os as _os
            if _os.environ.get("NOBCB"):
                P.op("tensor", lambda e: e.matmul(
                    PS_F[0:64, :], ones16_sb, rc16[0:1, 0:512],
                    start=True, stop=True),
                    waits=[W(c_dve)] + ([(s_sy, s_sy.tot)] if DEBUG == 2 else []),
                    incs=[(c_pe, 1)])
                P.op("vector", lambda e: e.tensor_copy(R_sb[0:64, :], PS_F[0:64, :]),
                     waits=[W(c_pe)], drain=True, incs=[(c_dve, 1)])
                P.op("tensor", lambda e: e.matmul(
                    PS_F[0:64, :], ones16_sb, rc16[0:1, 512:1024],
                    start=True, stop=True),
                    waits=[W(c_dve)], incs=[(c_pe, 1)])
                rc_free[0] = c_pe.tot
                P.op("vector", lambda e: e.tensor_copy(R_sb[64:128, :], PS_F[0:64, :]),
                     waits=[W(c_pe)], drain=True)
            else:
                def bcast(e):
                    e.matmul(PS_F[0:64, :], ones16_sb, rc16[0:1, 0:512],
                             start=True, stop=True, tile_position=(0, 0))
                    return e.matmul(PS_F[64:128, :], ones16_sb, rc16[0:1, 512:1024],
                                    start=True, stop=True, tile_position=(0, 64))
                P.op("tensor", bcast, waits=[W(c_dve)] +
                     ([(s_sy, s_sy.tot)] if DEBUG == 2 else []), incs=[(c_pe, 1)])
                rc_free[0] = c_pe.tot
                P.op("vector", lambda e: e.tensor_copy(R_sb, PS_F),
                     waits=[W(c_pe)], drain=True)
            P.op("vector", lambda e: e.tensor_tensor(
                o_sb[0:64, :], PS_C[0:64, :], R_sb[0:64, :], op=ALU.mult),
                waits=[(s_sy, o_free[0])], drain=True)
            P.op("vector", lambda e: e.tensor_tensor(
                o_sb[64:128, :], PS_D[0:64, :], R_sb[64:128, :], op=ALU.mult),
                drain=True, incs=[(c_dve, 1)])
            o_evac_prev[0] = c_dve.tot
            P.op("sync", lambda e, pair=pair, pos=pos: e.dma_start(
                out=o_scr[pair, pos], in_=o_sb),
                waits=[W(c_dve)], incs=[(s_sy, 16)])
            o_free[0] = s_sy.tot
            if DEBUG == 2:
                P.op("sync", lambda e, it=it: e.dma_start(
                    out=rdbg_h[it], in_=R_sb), waits=[W(c_dve)], incs=[(s_sy, 16)])
            import os as _os
            if not _os.environ.get("WQODD") or pair % 2 == 1:
                emit_wquant_chunk()
                emit_wquant_chunk()
                if _os.environ.get("WQODD"):
                    emit_wquant_chunk()
                    emit_wquant_chunk()
    while wq_i[0] < len(wq_items):
        emit_wquant_chunk()
    p2_pe = c_pe.tot
    p2_sy = s_sy.tot
    p2_gp = s_gp.tot

    if DEBUG == 2:
        P.op("sync", lambda e: e.dma_start(out=k2dbg_h, in_=K_fm),
             waits=[(c_pe, c_pe.tot), (c_dve, c_dve.tot)], incs=[(s_sy, 16)])
        P.op("sync", lambda e: e.dma_start(out=v2dbg_h, in_=V_aug), incs=[(s_sy, 16)])
        P.op("sync", lambda e: e.nop(), waits=[(s_sy, s_sy.tot), (s_gp, s_gp.tot)])
        return nc, P
    if DEBUG == 3:
        P.op("sync", lambda e: e.nop(),
             waits=[(c_pe, c_pe.tot), (c_dve, c_dve.tot), (c_act, c_act.tot),
                    (s_sy, s_sy.tot), (s_gp, s_gp.tot), (s_sc, s_sc.tot)])
        return nc, P

    P.op("gpsimd", lambda e: e.dma_start(out=wfcq_fm, in_=wfcq_scr),
         waits=[(c_pe, p2_pe), (s_gp, p2_gp)], incs=[(s_gp, 16)])
    P.op("gpsimd", lambda e: e.dma_start(out=wpjq_fm, in_=wpjq_scr),
         incs=[(s_gp, 16)])
    wres_rdy = s_gp.tot
    for lo in (0, 2048, 4096):
        P.op("scalar", lambda e, lo=lo: e.dma_start(
            out=rows32, in_=rows_h[0:1, lo:lo + 2048]),
            waits=([(c_dve, c_dve.tot), (s_gp, p2_gp)] if lo == 0 else []),
            incs=[(s_sc, 16)])
        P.op("vector", lambda e, lo=lo: e.tensor_copy(
            WA[0:1, 8192 + lo:8192 + lo + 2048], rows32),
            waits=[(s_sc, s_sc.tot)], drain=True, incs=[(c_dve, 1)])
    wld_dve[0] = wld_dve[1] = c_dve.tot
    for dc in range(8):
        sl = dc % 2
        P.op("gpsimd", lambda e, dc=dc, sl=sl: e.dma_start(
            out=gw_sb[:, sl * 1024:sl * 1024 + 1024],
            in_=woutT_h[128 * dc:128 * dc + 128, :]),
            waits=[(c_dve, wld_dve[sl]), (c_pe, p2_pe)], incs=[(s_gp, 16)])
        P.op("vector", lambda e, dc=dc, sl=sl: e.tensor_copy(
            woutT_bf[:, 1024 * dc:1024 * dc + 1024], gw_sb[:, sl * 1024:sl * 1024 + 1024]),
            waits=[(s_gp, s_gp.tot)], incs=[(c_dve, 1)])
        wld_dve[sl] = c_dve.tot
    wout_rdy = c_dve.tot
    _gf_init = c_dve.tot

    ot_free = [p2_pe, p2_pe]
    x1_free = [0]
    out_dma = [0]
    psb_free = [0]
    g_free = [_gf_init, _gf_init]
    for tt in range(NT_OWN):
        pos, sub = divmod(tt, 4)
        oslot = tt % 2
        xslot = tt % 2
        for pair in range(PAIRS):
            P.op("sync", lambda e, pair=pair, pos=pos, sub=sub, osl=oslot: e.dma_start(
                out=ot_sb[:, osl * 1024 + 128 * pair:osl * 1024 + 128 * (pair + 1)],
                in_=o_scr[pair, pos][:, 128 * sub:128 * sub + 128]),
                waits=([(c_pe, ot_free[oslot]), (s_sy, p2_sy)] if pair == 0 else []),
                incs=[(s_sy, 16)])
        o_rdy = s_sy.tot
        P.op("sync", lambda e, tt=tt, xs=xslot: e.dma_start(
            out=XO[xs], in_=x_q_h[128 * tt:128 * tt + 128, :]),
            waits=[(s_sy, out_dma[0]), (c_dve, c_dve.tot if tt < 2 else 0)],
            incs=[(s_sy, 16)])
        x_rdy = s_sy.tot
        def oproj(e, osl=oslot):
            last = None
            for dc in range(8):
                for half, bank in ((0, PS_C), (1, PS_D)):
                    last = e.matmul(bank,
                                    ot_sb[:, osl * 1024 + 128 * dc:osl * 1024 + 128 * (dc + 1)],
                                    woutT_bf[:, 1024 * dc + 512 * half:1024 * dc + 512 * (half + 1)],
                                    start=(dc == 0), stop=False)
            e.matmul(PS_C, ones_bf, bout_bf[0:1, 0:512], start=False, stop=True)
            return e.matmul(PS_D, ones_bf, bout_bf[0:1, 512:1024], start=False, stop=True)
        P.op("tensor", oproj,
             waits=[(s_sy, o_rdy), (c_dve, max(wout_rdy, c_dve.tot))],
             incs=[(c_pe, 1)])
        x1s = x1_sb
        P.op("vector", lambda e, x1=x1s, xs=xslot: e.tensor_add(
            x1[:, 0:512], PS_C, XO[xs][:, 0:512]),
            waits=[W(c_pe), (s_sy, x_rdy), (c_dve, x1_free[0])], drain=True)
        P.op("vector", lambda e, x1=x1s, xs=xslot: e.tensor_add(
            x1[:, 512:1024], PS_D, XO[xs][:, 512:1024]),
            drain=True, incs=[(c_dve, 1)])
        ln_tile(x1s, y_sb, tt % 2)
        y_rdy = c_act.tot
        P.op("vector", lambda e: e.engine_nop(), waits=[(c_act, y_rdy)])
        quant_chain(y_sb, tt % 2)
        quantT(tt % 2)
        P.op("vector", lambda e: e.tensor_copy(yq_sb, qout_sb),
             waits=[W(c_pe)], drain=True, incs=[(c_dve, 1)])
        for hc4 in range(4):
            gslot = hc4 % 2
            def fcmm(e, hc4=hc4):
                last = None
                for dc in range(8):
                    for half in range(2):
                        last = e.matmul(
                            PS_A[:, 512 * half:512 * half + 512],
                            yq_sb[:, 128 * dc:128 * dc + 128],
                            wfcq_fm[:, HID * dc + 1024 * hc4 + 512 * half:
                                    HID * dc + 1024 * hc4 + 512 * (half + 1)],
                            start=(dc == 0), stop=False)
                e.matmul(PS_A[:, 0:512], ones_bf,
                         bfc_bf[0:1, 1024 * hc4:1024 * hc4 + 512],
                         start=False, stop=True)
                return e.matmul(PS_A[:, 512:1024], ones_bf,
                                bfc_bf[0:1, 1024 * hc4 + 512:1024 * hc4 + 1024],
                                start=False, stop=True)
            P.op("tensor", fcmm,
                 waits=[(c_dve, c_dve.tot), (s_gp, wres_rdy), (c_act, c_act.tot)],
                 incs=[(c_pe, 1)])
            gs = gw_sb[:, gslot * 1024:gslot * 1024 + 1024]
            P.op("scalar", lambda e, g=gs: e.activation(g, PS_A, AF.Gelu),
                 waits=[W(c_pe), (c_dve, g_free[gslot])], incs=[(c_act, 1)])
            P.op("vector", lambda e: e.engine_nop(), waits=[W(c_act)])
            quant_chain(gs, gslot)
            g_free[gslot] = c_dve.tot
            quantT(gslot)
            P.op("vector", lambda e: e.tensor_copy(gqT_sb, qout_sb),
                 waits=[W(c_pe)], drain=True, incs=[(c_dve, 1)])
            def pjmm(e, hc4=hc4):
                last = None
                for hc in range(8):
                    ghc = 8 * hc4 + hc
                    for half in range(2):
                        last = e.matmul(
                            PS_B[:, 512 * half:512 * half + 512],
                            gqT_sb[:, 128 * hc:128 * hc + 128],
                            wpjq_fm[:, 1024 * ghc + 512 * half:1024 * ghc + 512 * (half + 1)],
                            start=(hc4 == 0 and hc == 0), stop=False)
                if hc4 == 3:
                    e.matmul(PS_B[:, 0:512], ones_bf, bpj_bf[0:1, 0:512],
                             start=False, stop=True)
                    last = e.matmul(PS_B[:, 512:1024], ones_bf, bpj_bf[0:1, 512:1024],
                                    start=False, stop=True)
                return last
            waits = [(c_dve, c_dve.tot)]
            if hc4 == 0:
                waits.append((c_dve, psb_free[0]))
            P.op("tensor", pjmm, waits=waits, incs=[(c_pe, 1)])
        pj_done = c_pe.tot
        ot_free[oslot] = pj_done
        P.op("vector", lambda e, xs=xslot, x1=x1s: e.tensor_add(XO[xs], PS_B, x1),
             waits=[(c_pe, pj_done)], drain=True, incs=[(c_dve, 1)])
        psb_free[0] = c_dve.tot
        x1_free[0] = c_dve.tot
        P.op("sync", lambda e, tt=tt, xs=xslot: e.dma_start(
            out=out_h[128 * tt:128 * tt + 128, :], in_=XO[xs]),
            waits=[W(c_dve)], incs=[(s_sy, 16)])
        out_dma[0] = s_sy.tot
    P.op("sync", lambda e: e.nop(), waits=[(s_sy, s_sy.tot)])
    return nc, P


def finish(nc, P):
    with nc.allow_low_precision(reason="bf16/fp16 operand prep and the fp16 "
                                "rounding ladder are part of the quantized-"
                                "module design; accumulation stays fp32"), \
         nc.Block() as block:
        @block.sync
        def _(eng):
            P.emit("sync", eng)

        @block.vector
        def _(eng):
            P.emit("vector", eng)

        @block.scalar
        def _(eng):
            P.emit("scalar", eng)

        @block.tensor
        def _(eng):
            P.emit("tensor", eng)

        @block.gpsimd
        def _(eng):
            P.emit("gpsimd", eng)
    return nc


def host_inputs(inputs):
    x = np.asarray(inputs["L0_attn_x"], dtype=np.float32)
    wqkv = np.asarray(inputs["L0_attn_Wqkv"], dtype=np.float32)
    wout = np.asarray(inputs["L0_attn_Wout"], dtype=np.float32)
    bout = np.asarray(inputs["L0_attn_bout"], dtype=np.float32)
    wfc = np.asarray(inputs["L0_ffn_w_fc"], dtype=np.float32)
    bfc = np.asarray(inputs["L0_ffn_b_fc"], dtype=np.float32)
    wpj = np.asarray(inputs["L0_ffn_w_proj"], dtype=np.float32)
    bpj = np.asarray(inputs["L0_ffn_b_proj"], dtype=np.float32)

    wqkvT = np.ascontiguousarray(wqkv.T)
    woutT = np.ascontiguousarray(wout.T)
    ident = np.eye(128, dtype=ml_dtypes.bfloat16)
    eind = np.zeros((32, 1024), dtype=ml_dtypes.bfloat16)
    for c in range(8):
        for pp in range(128):
            eind[4 * c + pp // 32, 128 * c + pp] = 1.0
    rowsin = np.zeros((1, 8192), dtype=np.float32)
    rowsin[0, 0:1024] = bout
    rowsin[0, 1024:5120] = bfc
    rowsin[0, 5120:6144] = bpj
    rowsin[0, 6144:6336] = 1.0

    in_maps = []
    rows = np.arange(128)[:, None]
    cols = np.arange(512)[None, :]
    for c in range(8):
        b = c // 2
        qcs = QCS_EVEN if c % 2 == 0 else QCS_ODD
        own_idx = np.concatenate([np.arange(qc * 512, qc * 512 + 512) for qc in qcs])
        masks = np.zeros((128, 4 * 4096), dtype=ml_dtypes.bfloat16)
        for pidx, qc in enumerate(qcs):
            qs = qc * 512
            E = POS_E[pidx]
            for jw in range(8):
                j = E - 8 + jw
                m = ((128 * j + rows) <= (qs + cols)).astype(np.float32)
                masks[:, 4096 * pidx + 512 * jw:4096 * pidx + 512 * (jw + 1)] = m
        in_maps.append(dict(
            x_kv=np.ascontiguousarray(x[b]),
            x_q=np.ascontiguousarray(x[b][own_idx]),
            wqkvT=wqkvT, woutT=woutT,
            bout=bout.reshape(1, D), wfc=wfc, bfc=bfc.reshape(1, HID),
            wpj=wpj, bpj=bpj.reshape(1, D),
            masks=masks, ident=ident, eind=eind, rowsin=rowsin,
        ))
    return in_maps


def kernel(**inputs):
    nc, P = build_program()
    finish(nc, P)
    in_maps = host_inputs(inputs)
    res = run_bass_kernel_spmd(nc, in_maps, core_ids=list(range(8)))
    out = np.zeros((B, T, D), dtype=np.float32)
    for c in range(8):
        b = c // 2
        qcs = QCS_EVEN if c % 2 == 0 else QCS_ODD
        own_idx = np.concatenate([np.arange(qc * 512, qc * 512 + 512) for qc in qcs])
        out[b][own_idx] = np.asarray(res.results[c]["out"], dtype=np.float32)
    return out
